# version 23
# baseline (speedup 1.0000x reference)
import contextlib
import ml_dtypes
import numpy as np
import concourse.bass as bass
import concourse.mybir as mybir
import concourse.bass_utils as _bu
from concourse.bass_utils import run_bass_kernel_spmd

if not getattr(_bu, "_ldw_opt_patched", False):
    _orig_run_command = _bu.run_command
    def _run_command_ldw(cmd, *a, **kw):
        try:
            cmd = ["--enable-ldw-opt=true" if c == "--enable-ldw-opt=false" else c for c in cmd]
        except TypeError:
            pass
        return _orig_run_command(cmd, *a, **kw)
    _bu.run_command = _run_command_ldw
    _bu._ldw_opt_patched = True

F32 = mybir.dt.float32
BF16 = mybir.dt.bfloat16
ALU = mybir.AluOpType
AF = mybir.ActivationFunctionType
AX = mybir.AxisListType.X

D = 1024
KC = 8
IDIM = 2816
T = 2048
NT = 16
ISLICES = [(i * 512, min(512, IDIM - i * 512)) for i in range((IDIM + 511) // 512)]

QPOLY_SCALE = float(1.0 / (4.0 * np.sqrt(2.0)))
QPOLY_BIAS = float(np.sqrt(2.0) / 2.0)


class Plan:
    ENGINES = ("sync", "tensor", "vector", "scalar", "gpsimd")

    def __init__(self, nc):
        self.nc = nc
        self.ops = {e: [] for e in self.ENGINES}
        self.sem_handles = {}
        self.counts = {}
        self.last_wait = {}

    def add_sem(self, name, handle):
        self.sem_handles[name] = handle
        self.counts[name] = 0

    def op(self, engine, fn, inc=None, n=1, group=None):
        if inc is not None and n >= 16:
            gkey = ("g", inc)
            gid = group if group is not None else object()
            if self.__dict__.setdefault("_groups", {}).get(gkey) != gid:
                self._groups[gkey] = gid
                self.wait(engine, inc)
        if inc is not None:
            self.counts[inc] += n
        sem = self.sem_handles[inc] if inc is not None else None

        def run(eng, fn=fn, sem=sem, n=n):
            instr = fn(eng)
            if sem is not None:
                instr.then_inc(sem, n)
        self.ops[engine].append(run)
        return self.counts.get(inc)

    def wait(self, engine, sem, value=None):
        value = self.counts[sem] if value is None else value
        if value <= 0:
            return
        key = (engine, sem)
        if self.last_wait.get(key, 0) >= value:
            return
        self.last_wait[key] = value
        h = self.sem_handles[sem]
        self.ops[engine].append(lambda eng, h=h, v=value: eng.wait_ge(h, v))

    def barrier(self, engines=None, sems=None):
        engines = engines or self.ENGINES
        sems = sems or list(self.sem_handles)
        for e in engines:
            for s in sems:
                self.wait(e, s)

    def materialize(self, block):
        for ename in self.ENGINES:
            ops = self.ops[ename]
            fn = {"sync": block.sync, "tensor": block.tensor, "vector": block.vector,
                  "scalar": block.scalar, "gpsimd": block.gpsimd}[ename]

            def body(eng, ops=ops):
                for o in ops:
                    o(eng)
            fn(body)


def build_core_program(n_seq=2, phases=("f1", "attn", "f2"), debug_out=None):
    nc = bass.Bass("TRN2", target_bir_lowering=False, detect_race_conditions=False)
    NTOK = n_seq * T

    x_d = nc.dram_tensor("x", [NTOK, D], F32, kind="ExternalInput").ap()
    wqT_d = nc.dram_tensor("wqT", [D, 256], F32, kind="ExternalInput").ap()
    wkT_d = nc.dram_tensor("wkT", [D, 256], F32, kind="ExternalInput").ap()
    wvT_d = nc.dram_tensor("wvT", [D, D], F32, kind="ExternalInput").ap()
    woT_d = nc.dram_tensor("woT", [D, D], F32, kind="ExternalInput").ap()
    wg_d = nc.dram_tensor("wg", [IDIM, D], F32, kind="ExternalInput").ap()
    wu_d = nc.dram_tensor("wu", [IDIM, D], F32, kind="ExternalInput").ap()
    wd_d = nc.dram_tensor("wd", [D, IDIM], F32, kind="ExternalInput").ap()
    ident_d = nc.dram_tensor("ident", [128, 128], F32, kind="ExternalInput").ap()
    identb_d = nc.dram_tensor("identb", [128, 128], BF16, kind="ExternalInput").ap()
    triu_d = nc.dram_tensor("triu", [128, 128], F32, kind="ExternalInput").ap()
    out_d = nc.dram_tensor("out", [NTOK, D], F32, kind="ExternalOutput").ap()

    wgTqh_d = nc.dram_tensor("wgTqh", [D, IDIM], BF16, kind="Internal").ap()
    wgTql_d = nc.dram_tensor("wgTql", [D, IDIM], BF16, kind="Internal").ap()
    wuTqh_d = nc.dram_tensor("wuTqh", [D, IDIM], BF16, kind="Internal").ap()
    wuTql_d = nc.dram_tensor("wuTql", [D, IDIM], BF16, kind="Internal").ap()
    wdTqh_d = nc.dram_tensor("wdTqh", [IDIM, D], BF16, kind="Internal").ap()
    wdTql_d = nc.dram_tensor("wdTql", [IDIM, D], BF16, kind="Internal").ap()
    zp_d = nc.dram_tensor("zp", [T, D], F32, kind="Internal").ap()

    stack = contextlib.ExitStack()
    BASE = 17408

    def sb_at(name, shape, dtype, off):
        return nc.alloc_sbuf_tensor_at(name, shape, dtype, offset=BASE + off).ap()

    z3d = sb_at("s_z", [128, NT, D], F32, 0)
    z = z3d
    xnqTh = sb_at("s_xnqTh", [128, KC, T], BF16, 65536)
    xnqTl = sb_at("s_xnqTl", [128, KC, T], BF16, 98304)
    v65 = sb_at("s_v65", [128, NT, 520], F32, 65536)
    k32 = sb_at("s_k32", [128, 2, T], F32, 98816)
    q32 = sb_at("s_q32", [128, 2, T], F32, 115200)
    qst = sb_at("s_qst", [128, T], F32, 98816)
    strip_h = sb_at("s_striph", [128, 22, 128], BF16, 65536)
    strip_l = sb_at("s_stripl", [128, 22, 128], BF16, 71168)
    watt = sb_at("s_watt", [128, 12288], F32, 131584)
    wqb = watt[:, 0:KC * 128].rearrange("p (c t) -> p c t", c=KC)
    wkb = watt[:, KC * 128:2 * KC * 128].rearrange("p (c t) -> p c t", c=KC)
    wvb = watt[:, 2 * KC * 128:2 * KC * 128 + KC * 512].rearrange("p (c t) -> p c t", c=KC)
    wob = watt[:, 2 * KC * 128 + KC * 512:2 * KC * 128 + KC * 512 + 4 * D].rearrange("p (c t) -> p c t", c=4)
    wgb_h = sb_at("s_wgbh", [128, KC, 512], BF16, 131584)
    wgb_l = sb_at("s_wgbl", [128, KC, 512], BF16, 139776)
    wub_h = sb_at("s_wubh", [128, KC, 512], BF16, 147968)
    wub_l = sb_at("s_wubl", [128, KC, 512], BF16, 156160)
    wdb_h = sb_at("s_wdbh", [128, 4, D], BF16, 164352)
    wdb_l = sb_at("s_wdbl", [128, 4, D], BF16, 172544)
    y_t = sb_at("s_y", [128, D], F32, 180736)
    qh_b = sb_at("s_qhb", [128, D], BF16, 180736)
    ql_b = sb_at("s_qlb", [128, D], BF16, 182784)
    aa = sb_at("s_aa", [128, 8, 128], F32, 180736)
    a_t = sb_at("s_a", [128, D], F32, 184832)
    u_t = a_t[:, 0:D]
    triu_s = a_t[:, 0:128]
    scr4 = sb_at("s_scr4", [128, 3 * D], F32, 188928)
    qA = scr4[:, 0:D]
    qB = scr4[:, D:2 * D]
    qq = scr4[:, 2 * D:3 * D]
    kst = scr4[:, 0:T]
    zpstage = scr4[:, 2 * D:3 * D]
    hqh_b = sb_at("s_hqhb", [128, 512], BF16, 188928)
    hql_b = sb_at("s_hqlb", [128, 512], BF16, 189952)
    sg_t = sb_at("s_sg", [128, D], F32, 201216)
    ystage = sg_t[:, 0:512]
    small = sb_at("s_small", [128, 16], F32, 205312)
    bm_t = sb_at("s_bm", [128, 64], F32, 205376)
    sp_t = sb_at("s_sp", [128, 64], F32, 205632)
    rr_t = sb_at("s_rr", [128, 64], F32, 205888)
    stats = sb_at("s_stats", [128, 64], F32, 206144)
    yT_wide = sb_at("s_yTw", [128, KC, 128], F32, 206400)
    yT = yT_wide[:, 0:4, :]
    hqTh_b = sb_at("s_hqThb", [128, 4, 128], BF16, 208448)
    hqTl_b = sb_at("s_hqTlb", [128, 4, 128], BF16, 209472)
    ident_s = sb_at("s_ident", [128, 128], F32, 210496)
    identb_s = sb_at("s_identb", [128, 128], BF16, 211008)
    bconst = small[:, 8:9]
    hconst = small[:, 9:10]

    SUMS, SSQ, MEAN, NMEAN, VAR, IV, DEN, RCP = range(8)
    sm = lambda i: small[:, i:i + 1]

    pb = [stack.enter_context(nc.psum_tensor(f"p_b{i}", [128, 512], F32)) for i in range(6)]
    pbh = [stack.enter_context(nc.psum_tensor(f"p_h{i}", [128, 1024], BF16)) for i in range(2)]

    P = Plan(nc)
    for sname in ("ld", "gd", "tn", "ve", "sl"):
        P.add_sem(sname, stack.enter_context(nc.semaphore("sem_" + sname)))

    bank_cons = {i: [] for i in range(8)}


    def bank_write_ready(bi):
        for (s, c) in bank_cons[bi]:
            P.wait("tensor", s, c)
        bank_cons[bi] = []

    def bank_consumed(bi, sem):
        bank_cons[bi].append((sem, P.counts[sem]))

    def batched_ln_stats():
        P.wait("scalar", "ve")
        for t in range(NT):
            P.op("scalar", lambda e, t=t: e.activation(
                a_t[:], z[:, t, :], AF.Copy, accum_out=stats[:, t:t + 1]), inc="sl")
        P.wait("vector", "sl")
        P.op("vector", lambda e: e.tensor_scalar(
            stats[:, 16:32], stats[:, 0:16], 1.0 / D, None, op0=ALU.mult), inc="ve")
        P.op("vector", lambda e: e.tensor_scalar(
            stats[:, 32:48], stats[:, 0:16], -1.0 / D, None, op0=ALU.mult), inc="ve")
        P.wait("scalar", "ve")
        for t in range(NT):
            P.op("scalar", lambda e, t=t: e.activation(
                a_t[:], z[:, t, :], AF.Square, bias=stats[:, 32 + t:33 + t], scale=1.0,
                accum_out=stats[:, t:t + 1]), inc="sl")
        P.wait("vector", "sl")
        P.op("vector", lambda e: e.tensor_scalar(
            stats[:, 0:16], stats[:, 0:16], 1.0 / D, 1e-5, op0=ALU.mult, op1=ALU.add), inc="ve")
        P.wait("vector", "ve")
        P.op("vector", lambda e: e.reciprocal(stats[:, 48:64], stats[:, 0:16]), inc="ve")
        P.wait("vector", "ve")
        P.wait("scalar", "ve")
        P.op("scalar", lambda e: e.activation(
            stats[:, 48:64], stats[:, 48:64], AF.Sqrt), inc="sl")

    def st_mean(t):
        return stats[:, 16 + t:17 + t]

    def st_iv(t):
        return stats[:, 48 + t:49 + t]

    def ln_stats(src_ap):
        P.wait("scalar", "ve")
        P.op("scalar", lambda e, s=src_ap: e.activation(
            a_t[:], s, AF.Copy, accum_out=sm(SUMS)), inc="sl")
        P.wait("vector", "sl")
        P.op("vector", lambda e: e.tensor_scalar(
            sm(MEAN), sm(SUMS), 1.0 / D, None, op0=ALU.mult), inc="ve")
        P.op("vector", lambda e: e.tensor_scalar(
            sm(NMEAN), sm(SUMS), -1.0 / D, None, op0=ALU.mult), inc="ve")
        P.wait("scalar", "ve")
        P.op("scalar", lambda e, s=src_ap: e.activation(
            a_t[:], s, AF.Square, bias=sm(NMEAN), scale=1.0, accum_out=sm(SSQ)), inc="sl")
        P.wait("vector", "sl")
        P.op("vector", lambda e: e.tensor_scalar(
            sm(VAR), sm(SSQ), 1.0 / D, 1e-5, op0=ALU.mult, op1=ALU.add), inc="ve")
        P.wait("vector", "ve")
        P.op("vector", lambda e: e.reciprocal(sm(RCP), sm(VAR)), inc="ve")
        P.wait("scalar", "ve")
        P.op("scalar", lambda e: e.activation(sm(IV), sm(RCP), AF.Sqrt), inc="sl")

    def quant_core(ysrc, width, dst_ap, scale_src, iv_ap=None):
        nb = width // 16
        P.op("vector", lambda e, y=ysrc, w=width: e.scalar_tensor_tensor(
            a_t[:, 0:w], y, -1.0, y, op0=ALU.mult, op1=ALU.max), inc="ve")
        P.wait("scalar", "ve")
        P.op("scalar", lambda e, y=ysrc, w=width: e.activation(
            sg_t[:, 0:w], y, AF.Sign), inc="sl")
        a3 = a_t[:, 0:width].rearrange("p (b k) -> p b k", k=16)
        P.op("vector", lambda e, a3=a3, nb=nb: e.tensor_reduce(
            bm_t[:, 0:nb], a3, axis=AX, op=ALU.max), inc="ve")
        P.wait("vector", "ve")
        if scale_src == "ln":
            P.wait("vector", "sl")
            P.op("vector", lambda e, nb=nb, iv_ap=iv_ap: e.tensor_scalar(
                rr_t[:, 0:nb], bm_t[:, 0:nb], iv_ap, 1.0 / 6.0,
                op0=ALU.mult, op1=ALU.mult), inc="ve")
            P.op("vector", lambda e, nb=nb: e.tensor_scalar(
                sp_t[:, 0:nb], rr_t[:, 0:nb], 1e-12, None, op0=ALU.max), inc="ve")
        else:
            P.op("vector", lambda e, nb=nb: e.tensor_scalar(
                sp_t[:, 0:nb], bm_t[:, 0:nb], 1.0 / 6.0, 1e-12,
                op0=ALU.mult, op1=ALU.max), inc="ve")
        P.wait("vector", "ve")
        P.op("vector", lambda e, nb=nb: e.reciprocal(rr_t[:, 0:nb], sp_t[:, 0:nb]), inc="ve")
        P.wait("vector", "ve")
        if scale_src == "ln":
            P.op("vector", lambda e, nb=nb, iv_ap=iv_ap: e.tensor_scalar(
                rr_t[:, 0:nb], rr_t[:, 0:nb], iv_ap, None, op0=ALU.mult), inc="ve")
        rb = rr_t[:, 0:nb].rearrange("p (b o) -> p b o", o=1).broadcast_to([128, nb, 16])
        u3 = u_t[:, 0:width].rearrange("p (b k) -> p b k", k=16)
        P.wait("vector", "ve")
        P.op("vector", lambda e, rb=rb, a3=a3, u3=u3: e.tensor_tensor(
            u3, rb, a3, op=ALU.mult), inc="ve")
        uu = u_t[:, 0:width]
        P.op("vector", lambda e, u=uu, w=width: e.tensor_scalar(
            qA[:, 0:w], u, 2.5, None, op0=ALU.is_gt), inc="ve")
        for thr in (3.5, 5.0, 5.0):
            P.op("vector", lambda e, u=uu, w=width, t=thr: e.scalar_tensor_tensor(
                qA[:, 0:w], u, t, qA[:, 0:w], op0=ALU.is_gt, op1=ALU.add), inc="ve")
        P.op("vector", lambda e, w=width: e.tensor_scalar(
            qA[:, 0:w], qA[:, 0:w], 2.0, None, op0=ALU.mult), inc="ve")
        for thr in (0.25, 0.75, 1.25, 1.75):
            P.op("vector", lambda e, u=uu, w=width, t=thr: e.scalar_tensor_tensor(
                qA[:, 0:w], u, t, qA[:, 0:w], op0=ALU.is_gt, op1=ALU.add), inc="ve")
        P.op("vector", lambda e, nb=nb: e.tensor_scalar(
            sp_t[:, 0:nb], sp_t[:, 0:nb], 0.5, None, op0=ALU.mult), inc="ve")
        sb3 = sp_t[:, 0:nb].rearrange("p (b o) -> p b o", o=1).broadcast_to([128, nb, 16])
        sg3 = sg_t[:, 0:width].rearrange("p (b k) -> p b k", k=16)
        P.wait("vector", "sl")
        P.wait("vector", "ve")
        P.op("vector", lambda e, sb3=sb3, sg3=sg3: e.tensor_tensor(
            sg3, sb3, sg3, op=ALU.mult), inc="ve")
        P.op("vector", lambda e, d=dst_ap, w=width: e.tensor_tensor(
            d, qA[:, 0:w], sg_t[:, 0:w], op=ALU.mult), inc="ve")

    def quant_ln_tile(src_ap, dst_ap, t):
        P.wait("vector", "sl")
        P.op("vector", lambda e, s=src_ap, t=t: e.tensor_scalar(
            y_t[:], s, st_mean(t), None, op0=ALU.subtract), inc="ve")
        quant_core(y_t[:], D, dst_ap, "ln", iv_ap=st_iv(t))

    def ln_full(src_ap, t):
        P.wait("vector", "sl")
        P.op("vector", lambda e, s=src_ap, t=t: e.tensor_scalar(
            y_t[:], s, st_mean(t), st_iv(t), op0=ALU.subtract, op1=ALU.mult), inc="ve")

    def transpose_to(src, n_chunks, dst3, dst_col0=0, b0=4):
        for kc in range(n_chunks):
            bi = b0 + (kc % 2)
            bank_write_ready(bi)
            P.op("tensor", lambda e, s=src, kc=kc, bi=bi: e.transpose(
                pb[bi][:, 0:128], s[:, kc * 128:(kc + 1) * 128], ident_s[:]), inc="tn")
            P.wait("scalar", "tn")
            P.op("scalar", lambda e, kc=kc, bi=bi, d=dst3, c0=dst_col0: e.copy(
                d[:, kc, c0:c0 + 128], pb[bi][:, 0:128]), inc="sl")
            bank_consumed(bi, "sl")

    def emit_ham_keepalive(ve_from, ve_to, step=3):
        v = ve_from + step
        while v < ve_to:
            P.wait("tensor", "ve", v)
            P.op("tensor", lambda e: e.matmul(
                pb[5][0:8, 504:512], ident_s[0:1, 0:8], ident_s[0:1, 0:8],
                start=True, stop=True))
            v += step
    def transpose_bf16(src_b, n_chunks, dst3, dst_col0=0):
        for kc in range(n_chunks):
            bi = 6 + (kc % 2)
            bank_write_ready(bi)
            P.op("tensor", lambda e, s=src_b, kc=kc, bi=bi: e.transpose(
                pbh[bi - 6][:, 0:128], s[:, kc * 128:(kc + 1) * 128], identb_s[:]), inc="tn")
            P.wait("scalar", "tn")
            P.op("scalar", lambda e, kc=kc, bi=bi, d=dst3, c0=dst_col0: e.copy(
                d[:, kc, c0:c0 + 128], pbh[bi - 6][:, 0:128]), inc="sl")
            bank_consumed(bi, "sl")

    def split_hl(src_f32, width, dst_h, dst_l):
        P.op("vector", lambda e, s=src_f32, w=width, d=dst_h: e.tensor_copy(
            d[:, 0:w], s[:, 0:w]), inc="ve")
        P.op("vector", lambda e, s=src_f32, w=width, dh=dst_h, dl=dst_l: e.tensor_tensor(
            dl[:, 0:w], s[:, 0:w], dh[:, 0:w], op=ALU.subtract), inc="ve")

    P.op("sync", lambda e: e.dma_start(out=ident_s[:], in_=ident_d), inc="ld", n=16, group=("id",))
    P.op("sync", lambda e: e.dma_start(out=identb_s[:], in_=identb_d), inc="ld", n=16, group=("id",))
    P.wait("tensor", "ld")
    P.op("vector", lambda e: e.memset(bconst, QPOLY_BIAS), inc="ve")
    P.op("vector", lambda e: e.memset(hconst, 0.5), inc="ve")

    def preprocess_weight(w_dram, n_rows, k_dim, wTh_dram, wTl_dram):
        n_rt = n_rows // 128
        kcs = k_dim // 128
        w3 = w_dram.rearrange("(r p) k -> r p k", p=128)
        wT3h = wTh_dram.rearrange("(c p) n -> p c n", p=128)
        wT3l = wTl_dram.rearrange("(c p) n -> p c n", p=128)
        ksl = [(0, D)] if k_dim == D else [(0, D), (D, D), (2 * D, k_dim - 2 * D)]
        for rt in range(n_rt):
            P.wait("scalar", "gd")
            for (k0, kw) in ksl:
                P.barrier(engines=("sync",), sems=("tn",))
                P.op("sync", lambda e, rt=rt, k0=k0, kw=kw: e.dma_start(
                    out=qq[:, 0:kw], in_=w3[rt, :, k0:k0 + kw]), inc="ld", n=16)
                P.wait("vector", "ld")
                P.wait("scalar", "ld")
                ve0 = P.counts["ve"]
                quant_core(qq[:, 0:kw], kw, qq[:, 0:kw], "plain")
                P.wait("vector", "tn")
                split_hl(qq, kw, qh_b, ql_b)
                emit_ham_keepalive(ve0, P.counts["ve"])
                P.wait("tensor", "ve")
                for kc in range(kw // 128):
                    bi = 6 + (kc % 2)
                    bank_write_ready(bi)
                    P.op("tensor", lambda e, kc=kc, bi=bi: e.transpose(
                        pbh[bi - 6][:, 0:128], qh_b[:, kc * 128:(kc + 1) * 128], identb_s[:]), inc="tn")
                    P.wait("scalar", "tn")
                    P.op("scalar", lambda e, kc=kc, bi=bi, k0=k0: e.copy(
                        strip_h[:, k0 // 128 + kc, :], pbh[bi - 6][:, 0:128]), inc="sl")
                    bank_consumed(bi, "sl")
                for kc in range(kw // 128):
                    bi = 6 + (kc % 2)
                    bank_write_ready(bi)
                    P.op("tensor", lambda e, kc=kc, bi=bi: e.transpose(
                        pbh[bi - 6][:, 0:128], ql_b[:, kc * 128:(kc + 1) * 128], identb_s[:]), inc="tn")
                    P.wait("scalar", "tn")
                    P.op("scalar", lambda e, kc=kc, bi=bi, k0=k0: e.copy(
                        strip_l[:, k0 // 128 + kc, :], pbh[bi - 6][:, 0:128]), inc="sl")
                    bank_consumed(bi, "sl")
            P.wait("gpsimd", "sl")
            P.op("gpsimd", lambda e, rt=rt, wT3h=wT3h, kcs=kcs: e.dma_start(
                out=wT3h[:, :, rt * 128:(rt + 1) * 128], in_=strip_h[:, 0:kcs, :]), inc="gd", n=16,
                group=("st", id(wTh_dram), rt))
            P.op("gpsimd", lambda e, rt=rt, wT3l=wT3l, kcs=kcs: e.dma_start(
                out=wT3l[:, :, rt * 128:(rt + 1) * 128], in_=strip_l[:, 0:kcs, :]), inc="gd", n=16,
                group=("st", id(wTh_dram), rt))

    if "f1" in phases or "f2" in phases:
        preprocess_weight(wg_d, IDIM, D, wgTqh_d, wgTql_d)
        preprocess_weight(wu_d, IDIM, D, wuTqh_d, wuTql_d)
        preprocess_weight(wd_d, D, IDIM, wdTqh_d, wdTql_d)
    P.barrier(engines=("sync",), sems=("gd",))

    wgT3h = wgTqh_d.rearrange("(c p) n -> p c n", p=128)
    wgT3l = wgTql_d.rearrange("(c p) n -> p c n", p=128)
    wuT3h = wuTqh_d.rearrange("(c p) n -> p c n", p=128)
    wuT3l = wuTql_d.rearrange("(c p) n -> p c n", p=128)
    wdT3h = wdTqh_d.rearrange("(c p) n -> p c n", p=128)
    wdT3l = wdTql_d.rearrange("(c p) n -> p c n", p=128)
    x3 = x_d.rearrange("(n p) k -> n p k", p=128)
    out3 = out_d.rearrange("(n p) k -> n p k", p=128)
    zp3 = zp_d.rearrange("(n p) k -> n p k", p=128)

    def ffn(seq, first):
        if first:
            P.barrier(engines=("sync",), sems=("ve", "sl"))
            for t in range(NT):
                P.op("sync", lambda e, seq=seq, t=t: e.dma_start(
                    out=z[:, t, :], in_=x3[seq * NT + t]), inc="ld", n=16, group=("x", seq))
            P.wait("vector", "ld")
            P.wait("scalar", "ld")
        batched_ln_stats()

        def prep(t):
            P.wait("vector", "tn")
            ve0 = P.counts["ve"]
            quant_ln_tile(z[:, t, :], qq[:], t)
            split_hl(qq, D, qh_b, ql_b)
            emit_ham_keepalive(ve0, P.counts["ve"])
            P.wait("tensor", "ve")
            transpose_bf16(qh_b, KC, xnqTh, t * 128)
            transpose_bf16(ql_b, KC, xnqTl, t * 128)
        def emit_gateup(t, nw, slot):
            gb, ub = (0, 1) if t % 2 == 0 else (2, 3)
            bank_write_ready(gb)
            bank_write_ready(ub)
            ts_ = t * 128
            for kc in range(KC):
                first, last = (kc == 0), (kc == KC - 1)
                for (bk, xop, wop, st, sp_) in (
                        (gb, xnqTh, wgb_h, first, False),
                        (gb, xnqTh, wgb_l, False, False),
                        (gb, xnqTl, wgb_h, False, last),
                        (ub, xnqTh, wub_h, first, False),
                        (ub, xnqTh, wub_l, False, False),
                        (ub, xnqTl, wub_h, False, last)):
                    P.op("tensor", lambda e, kc=kc, ts_=ts_, nw=nw, bk=bk, xop=xop,
                         wop=wop, st=st, sp_=sp_: e.matmul(
                        pb[bk][:, 0:nw], xop[:, kc, ts_:ts_ + 128],
                        wop[:, kc, 0:nw], start=st, stop=sp_),
                        inc="tn" if (sp_ and bk == ub) else None)
            return P.counts["tn"]

        def emit_tail(t, nw, nkc4, gu_done, slot):
            gb, ub = (0, 1) if t % 2 == 0 else (2, 3)
            P.wait("scalar", "tn", gu_done)
            P.wait("scalar", "ve")
            P.op("scalar", lambda e, nw=nw, gb=gb: e.activation(
                a_t[:, 0:nw], pb[gb][:, 0:nw], AF.Silu), inc="sl")
            bank_consumed(gb, "sl")
            P.wait("vector", "sl")
            P.wait("vector", "tn", gu_done)
            P.op("vector", lambda e, nw=nw, ub=ub: e.tensor_tensor(
                y_t[:, 0:nw], a_t[:, 0:nw], pb[ub][:, 0:nw], op=ALU.mult), inc="ve")
            bank_consumed(ub, "ve")
            quant_core(y_t[:, 0:nw], nw, qB[:, 0:nw], "plain")
            split_hl(qB, nw, hqh_b, hql_b)
            P.wait("tensor", "ve")
            transpose_bf16(hqh_b, nkc4, hqTh_b, 0)
            transpose_bf16(hql_b, nkc4, hqTl_b, 0)
            bank_write_ready(4)
            bank_write_ready(5)
            P.wait("tensor", "sl")
            for kc in range(nkc4):
                first, last = (kc == 0), (kc == nkc4 - 1)
                for (bk, c0, hop, wop, st, sp_) in (
                        (4, 0, hqTh_b, wdb_h, first, False),
                        (4, 0, hqTh_b, wdb_l, False, False),
                        (4, 0, hqTl_b, wdb_h, False, last),
                        (5, 512, hqTh_b, wdb_h, first, False),
                        (5, 512, hqTh_b, wdb_l, False, False),
                        (5, 512, hqTl_b, wdb_h, False, last)):
                    P.op("tensor", lambda e, kc=kc, bk=bk, c0=c0, hop=hop, wop=wop,
                         st=st, sp_=sp_: e.matmul(
                        pb[bk][:, :], hop[:, kc, :], wop[:, kc, c0:c0 + 512],
                        start=st, stop=sp_),
                        inc="tn" if (sp_ and bk == 5) else None)
            P.wait("vector", "tn")
            P.op("vector", lambda e, t=t: e.scalar_tensor_tensor(
                z[:, t, 0:512], pb[4][:, :], 0.5, z[:, t, 0:512],
                op0=ALU.mult, op1=ALU.add), inc="ve")
            bank_consumed(4, "ve")
            P.op("vector", lambda e, t=t: e.scalar_tensor_tensor(
                z[:, t, 512:1024], pb[5][:, :], 0.5, z[:, t, 512:1024],
                op0=ALU.mult, op1=ALU.add), inc="ve")
            bank_consumed(5, "ve")

        def emit_wload(isl):
            n0, nw = ISLICES[isl]
            nkc4 = nw // 128
            P.barrier(engines=("sync",), sems=("tn",))
            for (dst, src3, wide) in ((wgb_h, wgT3h, nw), (wgb_l, wgT3l, nw),
                                      (wub_h, wuT3h, nw), (wub_l, wuT3l, nw)):
                P.op("sync", lambda e, dst=dst, src3=src3, n0=n0, wide=wide: e.dma_start(
                    out=dst[:, :, 0:wide], in_=src3[:, :, n0:n0 + wide]), inc="ld", n=16,
                    group=("wl", isl))
            for (dst, src3) in ((wdb_h, wdT3h), (wdb_l, wdT3l)):
                P.op("sync", lambda e, dst=dst, src3=src3, n0=n0, nkc4=nkc4: e.dma_start(
                    out=dst[:, 0:nkc4, :], in_=src3[:, n0 // 128:n0 // 128 + nkc4, :]), inc="ld", n=16,
                    group=("wl", isl))
            return P.counts["ld"]

        wload_done = emit_wload(0)
        for t in range(NT):
            prep(t)
        for isl, (n0, nw) in enumerate(ISLICES):
            nkc4 = nw // 128
            if isl > 0:
                wload_done = emit_wload(isl)
            P.wait("tensor", "ld", wload_done)
            P.wait("tensor", "sl")
            gu_done = emit_gateup(0, nw, 0)
            for t in range(NT):
                nxt = emit_gateup(t + 1, nw, 0) if t + 1 < NT else None
                emit_tail(t, nw, nkc4, gu_done, 0)
                gu_done = nxt

    def attention(seq):
        for hp in range(2):
            P.barrier()
            P.op("sync", lambda e, hp=hp: e.dma_start(
                out=wqb[:], in_=wqT_d.rearrange("(c p) n -> p c n", p=128)[:, :, hp * 128:(hp + 1) * 128]), inc="ld", n=16, group=("aw", seq, hp))
            P.op("sync", lambda e, hp=hp: e.dma_start(
                out=wkb[:], in_=wkT_d.rearrange("(c p) n -> p c n", p=128)[:, :, hp * 128:(hp + 1) * 128]), inc="ld", n=16, group=("aw", seq, hp))
            P.op("sync", lambda e, hp=hp: e.dma_start(
                out=wvb[:], in_=wvT_d.rearrange("(c p) n -> p c n", p=128)[:, :, hp * 512:(hp + 1) * 512]), inc="ld", n=16, group=("aw", seq, hp))
            P.op("sync", lambda e, hp=hp: e.dma_start(
                out=wob[:], in_=woT_d.rearrange("(c p) n -> p c n", p=128)[:, hp * 4:(hp + 1) * 4, :]), inc="ld", n=16, group=("aw", seq, hp))
            P.op("vector", lambda e: e.memset(
                v65[:].rearrange("p t (h u) -> p t h u", u=65)[:, :, :, 64:65], 1.0), inc="ve")
            P.wait("tensor", "ld")
            if hp == 0:
                batched_ln_stats()
            for t in range(NT):
                ln_full(z[:, t, :], t)
                P.wait("tensor", "ve")
                P.wait("tensor", "sl")
                transpose_to(y_t[:], KC, yT_wide, 0, b0=2)
                P.wait("tensor", "sl")
                bank_write_ready(0)
                for kc in range(KC):
                    P.op("tensor", lambda e, kc=kc: e.matmul(
                        pb[0][:, 0:128], wkb[:, kc, :], yT_wide[:, kc, :],
                        start=(kc == 0), stop=(kc == KC - 1)),
                        inc="tn" if kc == KC - 1 else None)
                P.wait("scalar", "tn")
                P.op("scalar", lambda e, t=t: e.copy(
                    kst[:, t * 128:(t + 1) * 128], pb[0][:, 0:128]), inc="sl")
                bank_consumed(0, "sl")
                bank_write_ready(0)
                for kc in range(KC):
                    P.op("tensor", lambda e, kc=kc: e.matmul(
                        pb[0][:, 0:128], wqb[:, kc, :], yT_wide[:, kc, :],
                        start=(kc == 0), stop=(kc == KC - 1)),
                        inc="tn" if kc == KC - 1 else None)
                P.wait("scalar", "tn")
                P.op("scalar", lambda e, t=t: e.copy(
                    qst[:, t * 128:(t + 1) * 128], pb[0][:, 0:128]), inc="sl")
                bank_consumed(0, "sl")
                bank_write_ready(1)
                for kc in range(KC):
                    P.op("tensor", lambda e, kc=kc: e.matmul(
                        pb[1][:, :], yT_wide[:, kc, :], wvb[:, kc, :],
                        start=(kc == 0), stop=(kc == KC - 1)),
                        inc="tn" if kc == KC - 1 else None)
                P.wait("scalar", "tn")
                v65v = v65[:, t, :].rearrange("p (h u) -> p h u", u=65)
                P.op("scalar", lambda e, v65v=v65v: e.copy(
                    v65v[:, :, 0:64], pb[1][:, :].rearrange("p (h u) -> p h u", u=64)), inc="sl")
                bank_consumed(1, "sl")
            P.wait("sync", "sl")
            for h in range(8):
                g, c = h % 4, h // 4
                P.op("sync", lambda e, h=h, g=g, c=c: e.dma_start(
                    out=q32[32 * g:32 * g + 16, c, :], in_=qst[h * 16:(h + 1) * 16, :]), inc="ld", n=16,
                    group=("rgq", seq, hp))
            P.wait("sync", "ld")
            for h in range(8):
                g, c = h % 4, h // 4
                P.op("sync", lambda e, h=h, g=g, c=c: e.dma_start(
                    out=k32[32 * g:32 * g + 16, c, :], in_=kst[h * 16:(h + 1) * 16, :]), inc="ld", n=16,
                    group=("rgk", seq, hp))
            P.wait("sync", "sl")
            P.wait("sync", "ve")
            P.op("sync", lambda e: e.dma_start(out=triu_s, in_=triu_d), inc="ld", n=16)
            P.wait("tensor", "ld")
            P.wait("vector", "ld")
            aa8 = aa

            def emit_wave_qk(g, c, ti, w):
                tjs = list(range(4 * w, min(4 * w + 4, ti + 1)))
                sb_ = w % 2
                bank_write_ready(sb_)
                for k, tj in enumerate(tjs):
                    P.op("tensor", lambda e, g=g, c=c, ti=ti, tj=tj, k=k, sb_=sb_: e.matmul(
                        pb[sb_][:, k * 128:(k + 1) * 128],
                        k32[32 * g:32 * g + 16, c, tj * 128:(tj + 1) * 128],
                        q32[32 * g:32 * g + 16, c, ti * 128:(ti + 1) * 128],
                        start=True, stop=True, tile_position=(32 * g, 0)),
                        inc="tn" if k == len(tjs) - 1 else None)
                return P.counts["tn"]

            def emit_wave_poly_av(ti, h, w, qk_done, nb_):
                tjs = list(range(4 * w, min(4 * w + 4, ti + 1)))
                sb_ = w % 2
                P.wait("scalar", "tn", qk_done)
                for k, tj in enumerate(tjs):
                    aslot = (w % 2) * 4 + k
                    P.op("scalar", lambda e, k=k, sb_=sb_, aslot=aslot: e.activation(
                        aa8[:, aslot, :], pb[sb_][:, k * 128:(k + 1) * 128], AF.Square,
                        bias=bconst, scale=QPOLY_SCALE),
                        inc="sl")
                bank_consumed(sb_, "sl")
                P.wait("vector", "sl")
                for k, tj in enumerate(tjs):
                    aslot = (w % 2) * 4 + k
                    if tj == ti:
                        P.op("vector", lambda e, aslot=aslot: e.scalar_tensor_tensor(
                            aa8[:, aslot, :], aa8[:, aslot, :], 0.5, triu_s,
                            op0=ALU.add, op1=ALU.mult),
                            inc="ve" if k == len(tjs) - 1 else None)
                    else:
                        P.op("vector", lambda e, aslot=aslot: e.tensor_scalar(
                            aa8[:, aslot, :], aa8[:, aslot, :], 0.5, None, op0=ALU.add),
                            inc="ve" if k == len(tjs) - 1 else None)
                P.wait("tensor", "ve")
                for k, tj in enumerate(tjs):
                    aslot = (w % 2) * 4 + k
                    P.op("tensor", lambda e, tj=tj, ti=ti, h=h, nb_=nb_, aslot=aslot: e.matmul(
                        pb[nb_][:, 0:65], aa8[:, aslot, :],
                        v65[:, tj, h * 65:(h + 1) * 65],
                        start=(tj == 0), stop=(tj == ti)),
                        inc="tn" if tj == ti else None)

            for ti in range(NT):
                for hg in range(2):
                    for hh_ in range(4):
                        h = hg * 4 + hh_
                        g, c = h % 4, h // 4
                        nb_ = 2 + hh_
                        bank_write_ready(nb_)
                        nwv = (ti + 4) // 4
                        qk_done = emit_wave_qk(g, c, ti, 0)
                        for w in range(nwv):
                            nxt = emit_wave_qk(g, c, ti, w + 1) if w + 1 < nwv else None
                            emit_wave_poly_av(ti, h, w, qk_done, nb_)
                            qk_done = nxt
                    P.wait("vector", "tn")
                    for hh_ in range(4):
                        P.op("vector", lambda e, hh_=hh_: e.tensor_scalar(
                            small[:, 10 + hh_:11 + hh_], pb[2 + hh_][:, 64:65], 1e-12,
                            None, op0=ALU.add), inc="ve")
                    P.wait("vector", "ve")
                    P.op("vector", lambda e: e.reciprocal(bm_t[:, 0:4], small[:, 10:14]), inc="ve")
                    P.wait("vector", "ve")
                    for hh_ in range(4):
                        h = hg * 4 + hh_
                        P.op("vector", lambda e, hh_=hh_, h=h: e.tensor_scalar(
                            ystage[:, h * 64:(h + 1) * 64], pb[2 + hh_][:, 0:64],
                            bm_t[:, hh_:hh_ + 1], None, op0=ALU.mult), inc="ve")
                        bank_consumed(2 + hh_, "ve")
                P.wait("tensor", "ve")
                transpose_to(ystage[:], 4, yT, 0)
                bank_write_ready(4)
                bank_write_ready(5)
                P.wait("tensor", "sl")
                for kc in range(4):
                    P.op("tensor", lambda e, kc=kc: e.matmul(
                        pb[4][:, :], yT[:, kc, :], wob[:, kc, 0:512],
                        start=(kc == 0), stop=(kc == 3)), inc="tn" if kc == 3 else None)
                for kc in range(4):
                    P.op("tensor", lambda e, kc=kc: e.matmul(
                        pb[5][:, :], yT[:, kc, :], wob[:, kc, 512:1024],
                        start=(kc == 0), stop=(kc == 3)), inc="tn" if kc == 3 else None)
                if hp == 0:
                    P.wait("vector", "tn")
                    P.wait("vector", "gd")
                    P.op("vector", lambda e: e.tensor_copy(zpstage[:, 0:512], pb[4][:, :]), inc="ve")
                    bank_consumed(4, "ve")
                    P.op("vector", lambda e: e.tensor_copy(zpstage[:, 512:1024], pb[5][:, :]), inc="ve")
                    bank_consumed(5, "ve")
                    P.wait("gpsimd", "ve")
                    P.op("gpsimd", lambda e, ti=ti: e.dma_start(
                        out=zp3[ti], in_=zpstage), inc="gd", n=16)
                else:
                    P.wait("sync", "ve")
                    P.op("sync", lambda e, ti=ti: e.dma_start(
                        out=zpstage, in_=zp3[ti]), inc="ld", n=16)
                    P.wait("vector", "tn")
                    P.wait("vector", "ld")
                    P.op("vector", lambda e, ti=ti: e.tensor_tensor(
                        z[:, ti, 0:512], z[:, ti, 0:512], pb[4][:, :], op=ALU.add), inc="ve")
                    bank_consumed(4, "ve")
                    P.op("vector", lambda e, ti=ti: e.tensor_tensor(
                        z[:, ti, 512:1024], z[:, ti, 512:1024], pb[5][:, :], op=ALU.add), inc="ve")
                    bank_consumed(5, "ve")
                    P.op("vector", lambda e, ti=ti: e.tensor_tensor(
                        z[:, ti, :], z[:, ti, :], zpstage, op=ALU.add), inc="ve")

    def dump_z(seq):
        P.barrier(engines=("gpsimd",))
        for t in range(NT):
            P.op("gpsimd", lambda e, seq=seq, t=t: e.dma_start(
                out=out3[seq * NT + t], in_=z[:, t, :]), inc="gd", n=16, group=("zo", seq))

    def dump_xnq():
        P.barrier(engines=("gpsimd",))
        av = arenaA[:, 0:KC * T].rearrange("p (c t) -> p c t", c=NT)
        for i in range(NT):
            P.op("gpsimd", lambda e, i=i: e.dma_start(
                out=out3[i], in_=av[:, i, :]), inc="gd", n=16)

    def dump_wgtq(w_dram):
        P.barrier(engines=("gpsimd",))
        w3v = w_dram.rearrange("(c p) n -> c p n", p=128)
        nch = w3v.shape[0]
        ww = w3v.shape[2]
        ndump = NT * n_seq
        njp = (ww + D - 1) // D
        for c in range(nch):
            if (c + 1) * njp > ndump:
                break
            for j in range((ww + D - 1) // D):
                wj = min(D, ww - j * D)
                P.op("gpsimd", lambda e, c=c, j=j, wj=wj, w3v=w3v, njp=njp: e.dma_start(
                    out=out3[c * njp + j][:, 0:wj], in_=w3v[c, :, j * D:j * D + wj]), inc="gd", n=16)

    for seq in range(n_seq):
        P.barrier()
        if "f1" in phases:
            ffn(seq, first=True)
        else:
            for t in range(NT):
                P.op("sync", lambda e, seq=seq, t=t: e.dma_start(
                    out=z[:, t, :], in_=x3[seq * NT + t]), inc="ld", n=16, group=("x", seq))
            P.barrier()
        if debug_out == "z1":
            dump_z(seq)
            continue
        if debug_out in ("xnq", "wgtq", "wdtq") and seq == 0:
            if debug_out == "xnq":
                dump_xnq()
            elif debug_out == "wgtq":
                dump_wgtq(wgTq_d)
            else:
                dump_wgtq(wdTq_d)
            break
        if "attn" in phases:
            attention(seq)
        if debug_out == "z2":
            dump_z(seq)
            continue
        P.barrier()
        if "f2" in phases:
            ffn(seq, first=False)
        dump_z(seq)

    P.wait("gpsimd", "gd")

    with nc.Block() as block:
        P.materialize(block)
    stack.close()
    return nc


def kernel(L0_attn_hidden_states, L0_attn_Wq, L0_attn_Wk, L0_attn_Wv, L0_attn_Wo,
           L0_ffn_w_gate, L0_ffn_w_up, L0_ffn_w_down,
           _phases=("f1", "attn", "f2"), _debug_out=None):
    x = np.ascontiguousarray(np.asarray(L0_attn_hidden_states, dtype=np.float32))
    B, L, Dm = x.shape
    n_cores = 8
    spc = B // n_cores
    common = {
        "wqT": np.ascontiguousarray(np.asarray(L0_attn_Wq, np.float32).T),
        "wkT": np.ascontiguousarray(np.asarray(L0_attn_Wk, np.float32).T),
        "wvT": np.ascontiguousarray(np.asarray(L0_attn_Wv, np.float32).T),
        "woT": np.ascontiguousarray(np.asarray(L0_attn_Wo, np.float32).T),
        "wg": np.ascontiguousarray(np.asarray(L0_ffn_w_gate, np.float32)),
        "wu": np.ascontiguousarray(np.asarray(L0_ffn_w_up, np.float32)),
        "wd": np.ascontiguousarray(np.asarray(L0_ffn_w_down, np.float32)),
        "ident": np.eye(128, dtype=np.float32),
        "identb": np.eye(128).astype(ml_dtypes.bfloat16),
        "triu": np.triu(np.ones((128, 128), dtype=np.float32)),
    }
    nc = build_core_program(n_seq=spc, phases=_phases, debug_out=_debug_out)
    in_maps = []
    for c in range(n_cores):
        m = dict(common)
        m["x"] = np.ascontiguousarray(x[c * spc:(c + 1) * spc].reshape(spc * L, Dm))
        in_maps.append(m)
    res = run_bass_kernel_spmd(nc, in_maps, core_ids=list(range(n_cores)))
    outs = [np.asarray(res.results[c]["out"]).reshape(spc, L, Dm) for c in range(n_cores)]
    return np.concatenate(outs, axis=0)


# version 24 (final)
# speedup vs baseline: 1.0081x; 1.0081x over previous
import contextlib
import ml_dtypes
import numpy as np
import concourse.bass as bass
import concourse.mybir as mybir
import concourse.bass_utils as _bu
from concourse.bass_utils import run_bass_kernel_spmd

if not getattr(_bu, "_ldw_opt_patched", False):
    _orig_run_command = _bu.run_command
    def _run_command_ldw(cmd, *a, **kw):
        try:
            cmd = ["--enable-ldw-opt=true" if c == "--enable-ldw-opt=false" else c for c in cmd]
        except TypeError:
            pass
        return _orig_run_command(cmd, *a, **kw)
    _bu.run_command = _run_command_ldw
    _bu._ldw_opt_patched = True

F32 = mybir.dt.float32
BF16 = mybir.dt.bfloat16
ALU = mybir.AluOpType
AF = mybir.ActivationFunctionType
AX = mybir.AxisListType.X

D = 1024
KC = 8
IDIM = 2816
T = 2048
NT = 16
ISLICES = [(i * 512, min(512, IDIM - i * 512)) for i in range((IDIM + 511) // 512)]

QPOLY_SCALE = float(1.0 / (4.0 * np.sqrt(2.0)))
QPOLY_BIAS = float(np.sqrt(2.0) / 2.0)


class Plan:
    ENGINES = ("sync", "tensor", "vector", "scalar", "gpsimd")

    def __init__(self, nc):
        self.nc = nc
        self.ops = {e: [] for e in self.ENGINES}
        self.sem_handles = {}
        self.counts = {}
        self.last_wait = {}

    def add_sem(self, name, handle):
        self.sem_handles[name] = handle
        self.counts[name] = 0

    def op(self, engine, fn, inc=None, n=1, group=None):
        if inc is not None and n >= 16:
            gkey = ("g", inc)
            gid = group if group is not None else object()
            if self.__dict__.setdefault("_groups", {}).get(gkey) != gid:
                self._groups[gkey] = gid
                self.wait(engine, inc)
        if inc is not None:
            self.counts[inc] += n
        sem = self.sem_handles[inc] if inc is not None else None

        def run(eng, fn=fn, sem=sem, n=n):
            instr = fn(eng)
            if sem is not None:
                instr.then_inc(sem, n)
        self.ops[engine].append(run)
        return self.counts.get(inc)

    def wait(self, engine, sem, value=None):
        value = self.counts[sem] if value is None else value
        if value <= 0:
            return
        key = (engine, sem)
        if self.last_wait.get(key, 0) >= value:
            return
        self.last_wait[key] = value
        h = self.sem_handles[sem]
        self.ops[engine].append(lambda eng, h=h, v=value: eng.wait_ge(h, v))

    def barrier(self, engines=None, sems=None):
        engines = engines or self.ENGINES
        sems = sems or list(self.sem_handles)
        for e in engines:
            for s in sems:
                self.wait(e, s)

    def materialize(self, block):
        for ename in self.ENGINES:
            ops = self.ops[ename]
            fn = {"sync": block.sync, "tensor": block.tensor, "vector": block.vector,
                  "scalar": block.scalar, "gpsimd": block.gpsimd}[ename]

            def body(eng, ops=ops):
                for o in ops:
                    o(eng)
            fn(body)


def build_core_program(n_seq=2, phases=("f1", "attn", "f2"), debug_out=None):
    nc = bass.Bass("TRN2", target_bir_lowering=False, detect_race_conditions=False)
    NTOK = n_seq * T

    x_d = nc.dram_tensor("x", [NTOK, D], F32, kind="ExternalInput").ap()
    wqT_d = nc.dram_tensor("wqT", [D, 256], F32, kind="ExternalInput").ap()
    wkT_d = nc.dram_tensor("wkT", [D, 256], F32, kind="ExternalInput").ap()
    wvT_d = nc.dram_tensor("wvT", [D, D], F32, kind="ExternalInput").ap()
    woT_d = nc.dram_tensor("woT", [D, D], F32, kind="ExternalInput").ap()
    wg_d = nc.dram_tensor("wg", [IDIM, D], F32, kind="ExternalInput").ap()
    wu_d = nc.dram_tensor("wu", [IDIM, D], F32, kind="ExternalInput").ap()
    wd_d = nc.dram_tensor("wd", [D, IDIM], F32, kind="ExternalInput").ap()
    ident_d = nc.dram_tensor("ident", [128, 128], F32, kind="ExternalInput").ap()
    identb_d = nc.dram_tensor("identb", [128, 128], BF16, kind="ExternalInput").ap()
    triu_d = nc.dram_tensor("triu", [128, 128], F32, kind="ExternalInput").ap()
    out_d = nc.dram_tensor("out", [NTOK, D], F32, kind="ExternalOutput").ap()

    wgTqh_d = nc.dram_tensor("wgTqh", [D, IDIM], BF16, kind="Internal").ap()
    wgTql_d = nc.dram_tensor("wgTql", [D, IDIM], BF16, kind="Internal").ap()
    wuTqh_d = nc.dram_tensor("wuTqh", [D, IDIM], BF16, kind="Internal").ap()
    wuTql_d = nc.dram_tensor("wuTql", [D, IDIM], BF16, kind="Internal").ap()
    wdTqh_d = nc.dram_tensor("wdTqh", [IDIM, D], BF16, kind="Internal").ap()
    wdTql_d = nc.dram_tensor("wdTql", [IDIM, D], BF16, kind="Internal").ap()
    zp_d = nc.dram_tensor("zp", [T, D], F32, kind="Internal").ap()

    stack = contextlib.ExitStack()
    BASE = 17408

    def sb_at(name, shape, dtype, off):
        return nc.alloc_sbuf_tensor_at(name, shape, dtype, offset=BASE + off).ap()

    z3d = sb_at("s_z", [128, NT, D], F32, 0)
    z = z3d
    xnqTh = sb_at("s_xnqTh", [128, KC, T], BF16, 65536)
    xnqTl = sb_at("s_xnqTl", [128, KC, T], BF16, 98304)
    v65 = sb_at("s_v65", [128, NT, 520], F32, 65536)
    k32 = sb_at("s_k32", [128, 2, T], F32, 98816)
    q32 = sb_at("s_q32", [128, 2, T], F32, 115200)
    qst = sb_at("s_qst", [128, T], F32, 98816)
    strip_h = sb_at("s_striph", [128, 22, 128], BF16, 65536)
    strip_l = sb_at("s_stripl", [128, 22, 128], BF16, 71168)
    watt = sb_at("s_watt", [128, 12288], F32, 131584)
    wqb = watt[:, 0:KC * 128].rearrange("p (c t) -> p c t", c=KC)
    wkb = watt[:, KC * 128:2 * KC * 128].rearrange("p (c t) -> p c t", c=KC)
    wvb = watt[:, 2 * KC * 128:2 * KC * 128 + KC * 512].rearrange("p (c t) -> p c t", c=KC)
    wob = watt[:, 2 * KC * 128 + KC * 512:2 * KC * 128 + KC * 512 + 4 * D].rearrange("p (c t) -> p c t", c=4)
    wgb_h = sb_at("s_wgbh", [128, KC, 512], BF16, 131584)
    wgb_l = sb_at("s_wgbl", [128, KC, 512], BF16, 139776)
    wub_h = sb_at("s_wubh", [128, KC, 512], BF16, 147968)
    wub_l = sb_at("s_wubl", [128, KC, 512], BF16, 156160)
    wdb_h = sb_at("s_wdbh", [128, 4, D], BF16, 164352)
    wdb_l = sb_at("s_wdbl", [128, 4, D], BF16, 172544)
    y_t = sb_at("s_y", [128, D], F32, 180736)
    qh_b = sb_at("s_qhb", [128, D], BF16, 180736)
    ql_b = sb_at("s_qlb", [128, D], BF16, 182784)
    aa = sb_at("s_aa", [128, 8, 128], F32, 180736)
    a_t = sb_at("s_a", [128, D], F32, 184832)
    u_t = a_t[:, 0:D]
    triu_s = a_t[:, 0:128]
    scr4 = sb_at("s_scr4", [128, 3 * D], F32, 188928)
    qA = scr4[:, 0:D]
    qB = scr4[:, D:2 * D]
    qq = scr4[:, 2 * D:3 * D]
    kst = scr4[:, 0:T]
    zpstage = scr4[:, 2 * D:3 * D]
    hqh_b = sb_at("s_hqhb", [128, 512], BF16, 188928)
    hql_b = sb_at("s_hqlb", [128, 512], BF16, 189952)
    sg_t = sb_at("s_sg", [128, D], F32, 201216)
    ystage = sg_t[:, 0:512]
    small = sb_at("s_small", [128, 16], F32, 205312)
    bm_t = sb_at("s_bm", [128, 64], F32, 205376)
    sp_t = sb_at("s_sp", [128, 64], F32, 205632)
    rr_t = sb_at("s_rr", [128, 64], F32, 205888)
    stats = sb_at("s_stats", [128, 64], F32, 206144)
    yT_wide = sb_at("s_yTw", [128, KC, 128], F32, 206400)
    yT = yT_wide[:, 0:4, :]
    hqTh_b = sb_at("s_hqThb", [128, 4, 128], BF16, 208448)
    hqTl_b = sb_at("s_hqTlb", [128, 4, 128], BF16, 209472)
    ident_s = sb_at("s_ident", [128, 128], F32, 210496)
    identb_s = sb_at("s_identb", [128, 128], BF16, 211008)
    bconst = small[:, 8:9]
    hconst = small[:, 9:10]

    SUMS, SSQ, MEAN, NMEAN, VAR, IV, DEN, RCP = range(8)
    sm = lambda i: small[:, i:i + 1]

    pb = [stack.enter_context(nc.psum_tensor(f"p_b{i}", [128, 512], F32)) for i in range(6)]
    pbh = [stack.enter_context(nc.psum_tensor(f"p_h{i}", [128, 1024], BF16)) for i in range(2)]

    P = Plan(nc)
    for sname in ("ld", "gd", "tn", "ve", "sl"):
        P.add_sem(sname, stack.enter_context(nc.semaphore("sem_" + sname)))

    bank_cons = {i: [] for i in range(8)}


    def bank_write_ready(bi):
        for (s, c) in bank_cons[bi]:
            P.wait("tensor", s, c)
        bank_cons[bi] = []

    def bank_consumed(bi, sem):
        bank_cons[bi].append((sem, P.counts[sem]))

    def batched_ln_stats():
        P.wait("scalar", "ve")
        for t in range(NT):
            P.op("scalar", lambda e, t=t: e.activation(
                a_t[:], z[:, t, :], AF.Copy, accum_out=stats[:, t:t + 1]), inc="sl")
        P.wait("vector", "sl")
        P.op("vector", lambda e: e.tensor_scalar(
            stats[:, 16:32], stats[:, 0:16], 1.0 / D, None, op0=ALU.mult), inc="ve")
        P.op("vector", lambda e: e.tensor_scalar(
            stats[:, 32:48], stats[:, 0:16], -1.0 / D, None, op0=ALU.mult), inc="ve")
        P.wait("scalar", "ve")
        for t in range(NT):
            P.op("scalar", lambda e, t=t: e.activation(
                a_t[:], z[:, t, :], AF.Square, bias=stats[:, 32 + t:33 + t], scale=1.0,
                accum_out=stats[:, t:t + 1]), inc="sl")
        P.wait("vector", "sl")
        P.op("vector", lambda e: e.tensor_scalar(
            stats[:, 0:16], stats[:, 0:16], 1.0 / D, 1e-5, op0=ALU.mult, op1=ALU.add), inc="ve")
        P.wait("vector", "ve")
        P.op("vector", lambda e: e.reciprocal(stats[:, 48:64], stats[:, 0:16]), inc="ve")
        P.wait("vector", "ve")
        P.wait("scalar", "ve")
        P.op("scalar", lambda e: e.activation(
            stats[:, 48:64], stats[:, 48:64], AF.Sqrt), inc="sl")

    def st_mean(t):
        return stats[:, 16 + t:17 + t]

    def st_iv(t):
        return stats[:, 48 + t:49 + t]

    def ln_stats(src_ap):
        P.wait("scalar", "ve")
        P.op("scalar", lambda e, s=src_ap: e.activation(
            a_t[:], s, AF.Copy, accum_out=sm(SUMS)), inc="sl")
        P.wait("vector", "sl")
        P.op("vector", lambda e: e.tensor_scalar(
            sm(MEAN), sm(SUMS), 1.0 / D, None, op0=ALU.mult), inc="ve")
        P.op("vector", lambda e: e.tensor_scalar(
            sm(NMEAN), sm(SUMS), -1.0 / D, None, op0=ALU.mult), inc="ve")
        P.wait("scalar", "ve")
        P.op("scalar", lambda e, s=src_ap: e.activation(
            a_t[:], s, AF.Square, bias=sm(NMEAN), scale=1.0, accum_out=sm(SSQ)), inc="sl")
        P.wait("vector", "sl")
        P.op("vector", lambda e: e.tensor_scalar(
            sm(VAR), sm(SSQ), 1.0 / D, 1e-5, op0=ALU.mult, op1=ALU.add), inc="ve")
        P.wait("vector", "ve")
        P.op("vector", lambda e: e.reciprocal(sm(RCP), sm(VAR)), inc="ve")
        P.wait("scalar", "ve")
        P.op("scalar", lambda e: e.activation(sm(IV), sm(RCP), AF.Sqrt), inc="sl")

    def quant_core(ysrc, width, dst_ap, scale_src, iv_ap=None):
        nb = width // 16
        P.op("vector", lambda e, y=ysrc, w=width: e.scalar_tensor_tensor(
            a_t[:, 0:w], y, -1.0, y, op0=ALU.mult, op1=ALU.max), inc="ve")
        P.wait("scalar", "ve")
        P.op("scalar", lambda e, y=ysrc, w=width: e.activation(
            sg_t[:, 0:w], y, AF.Sign), inc="sl")
        a3 = a_t[:, 0:width].rearrange("p (b k) -> p b k", k=16)
        P.op("vector", lambda e, a3=a3, nb=nb: e.tensor_reduce(
            bm_t[:, 0:nb], a3, axis=AX, op=ALU.max), inc="ve")
        P.wait("vector", "ve")
        if scale_src == "ln":
            P.wait("vector", "sl")
            P.op("vector", lambda e, nb=nb, iv_ap=iv_ap: e.tensor_scalar(
                rr_t[:, 0:nb], bm_t[:, 0:nb], iv_ap, 1.0 / 6.0,
                op0=ALU.mult, op1=ALU.mult), inc="ve")
            P.op("vector", lambda e, nb=nb: e.tensor_scalar(
                sp_t[:, 0:nb], rr_t[:, 0:nb], 1e-12, None, op0=ALU.max), inc="ve")
        else:
            P.op("vector", lambda e, nb=nb: e.tensor_scalar(
                sp_t[:, 0:nb], bm_t[:, 0:nb], 1.0 / 6.0, 1e-12,
                op0=ALU.mult, op1=ALU.max), inc="ve")
        P.wait("vector", "ve")
        P.op("vector", lambda e, nb=nb: e.reciprocal(rr_t[:, 0:nb], sp_t[:, 0:nb]), inc="ve")
        P.wait("vector", "ve")
        if scale_src == "ln":
            P.op("vector", lambda e, nb=nb, iv_ap=iv_ap: e.tensor_scalar(
                rr_t[:, 0:nb], rr_t[:, 0:nb], iv_ap, None, op0=ALU.mult), inc="ve")
        rb = rr_t[:, 0:nb].rearrange("p (b o) -> p b o", o=1).broadcast_to([128, nb, 16])
        u3 = u_t[:, 0:width].rearrange("p (b k) -> p b k", k=16)
        P.wait("vector", "ve")
        P.op("vector", lambda e, rb=rb, a3=a3, u3=u3: e.tensor_tensor(
            u3, rb, a3, op=ALU.mult), inc="ve")
        uu = u_t[:, 0:width]
        P.op("vector", lambda e, u=uu, w=width: e.tensor_scalar(
            qA[:, 0:w], u, 2.5, None, op0=ALU.is_gt), inc="ve")
        for thr in (3.5, 5.0, 5.0):
            P.op("vector", lambda e, u=uu, w=width, t=thr: e.scalar_tensor_tensor(
                qA[:, 0:w], u, t, qA[:, 0:w], op0=ALU.is_gt, op1=ALU.add), inc="ve")
        P.op("vector", lambda e, w=width: e.tensor_scalar(
            qA[:, 0:w], qA[:, 0:w], 2.0, None, op0=ALU.mult), inc="ve")
        for thr in (0.25, 0.75, 1.25, 1.75):
            P.op("vector", lambda e, u=uu, w=width, t=thr: e.scalar_tensor_tensor(
                qA[:, 0:w], u, t, qA[:, 0:w], op0=ALU.is_gt, op1=ALU.add), inc="ve")
        P.op("vector", lambda e, nb=nb: e.tensor_scalar(
            sp_t[:, 0:nb], sp_t[:, 0:nb], 0.5, None, op0=ALU.mult), inc="ve")
        sb3 = sp_t[:, 0:nb].rearrange("p (b o) -> p b o", o=1).broadcast_to([128, nb, 16])
        sg3 = sg_t[:, 0:width].rearrange("p (b k) -> p b k", k=16)
        P.wait("vector", "sl")
        P.wait("vector", "ve")
        P.op("vector", lambda e, sb3=sb3, sg3=sg3: e.tensor_tensor(
            sg3, sb3, sg3, op=ALU.mult), inc="ve")
        P.op("vector", lambda e, d=dst_ap, w=width: e.tensor_tensor(
            d, qA[:, 0:w], sg_t[:, 0:w], op=ALU.mult), inc="ve")

    def quant_ln_tile(src_ap, dst_ap, t):
        P.wait("vector", "sl")
        P.op("vector", lambda e, s=src_ap, t=t: e.tensor_scalar(
            y_t[:], s, st_mean(t), None, op0=ALU.subtract), inc="ve")
        quant_core(y_t[:], D, dst_ap, "ln", iv_ap=st_iv(t))

    def ln_full(src_ap, t):
        P.wait("vector", "sl")
        P.op("vector", lambda e, s=src_ap, t=t: e.tensor_scalar(
            y_t[:], s, st_mean(t), st_iv(t), op0=ALU.subtract, op1=ALU.mult), inc="ve")

    def transpose_to(src, n_chunks, dst3, dst_col0=0, b0=4):
        for kc in range(n_chunks):
            bi = b0 + (kc % 2)
            bank_write_ready(bi)
            P.op("tensor", lambda e, s=src, kc=kc, bi=bi: e.transpose(
                pb[bi][:, 0:128], s[:, kc * 128:(kc + 1) * 128], ident_s[:]), inc="tn")
            P.wait("scalar", "tn")
            P.op("scalar", lambda e, kc=kc, bi=bi, d=dst3, c0=dst_col0: e.copy(
                d[:, kc, c0:c0 + 128], pb[bi][:, 0:128]), inc="sl")
            bank_consumed(bi, "sl")

    def emit_ham_keepalive(ve_from, ve_to, step=10**9):
        v = ve_from + step
        while v < ve_to:
            P.wait("tensor", "ve", v)
            P.op("tensor", lambda e: e.matmul(
                pb[5][0:8, 504:512], ident_s[0:1, 0:8], ident_s[0:1, 0:8],
                start=True, stop=True))
            v += step
    def transpose_bf16(src_b, n_chunks, dst3, dst_col0=0):
        for kc in range(n_chunks):
            bi = 6 + (kc % 2)
            bank_write_ready(bi)
            P.op("tensor", lambda e, s=src_b, kc=kc, bi=bi: e.transpose(
                pbh[bi - 6][:, 0:128], s[:, kc * 128:(kc + 1) * 128], identb_s[:]), inc="tn")
            P.wait("scalar", "tn")
            P.op("scalar", lambda e, kc=kc, bi=bi, d=dst3, c0=dst_col0: e.copy(
                d[:, kc, c0:c0 + 128], pbh[bi - 6][:, 0:128]), inc="sl")
            bank_consumed(bi, "sl")

    def split_hl(src_f32, width, dst_h, dst_l):
        P.op("vector", lambda e, s=src_f32, w=width, d=dst_h: e.tensor_copy(
            d[:, 0:w], s[:, 0:w]), inc="ve")
        P.op("vector", lambda e, s=src_f32, w=width, dh=dst_h, dl=dst_l: e.tensor_tensor(
            dl[:, 0:w], s[:, 0:w], dh[:, 0:w], op=ALU.subtract), inc="ve")

    P.op("sync", lambda e: e.dma_start(out=ident_s[:], in_=ident_d), inc="ld", n=16, group=("id",))
    P.op("sync", lambda e: e.dma_start(out=identb_s[:], in_=identb_d), inc="ld", n=16, group=("id",))
    P.wait("tensor", "ld")
    P.op("vector", lambda e: e.memset(bconst, QPOLY_BIAS), inc="ve")
    P.op("vector", lambda e: e.memset(hconst, 0.5), inc="ve")

    def preprocess_weight(w_dram, n_rows, k_dim, wTh_dram, wTl_dram):
        n_rt = n_rows // 128
        kcs = k_dim // 128
        w3 = w_dram.rearrange("(r p) k -> r p k", p=128)
        wT3h = wTh_dram.rearrange("(c p) n -> p c n", p=128)
        wT3l = wTl_dram.rearrange("(c p) n -> p c n", p=128)
        ksl = [(0, D)] if k_dim == D else [(0, D), (D, D), (2 * D, k_dim - 2 * D)]
        for rt in range(n_rt):
            P.wait("scalar", "gd")
            for (k0, kw) in ksl:
                P.barrier(engines=("sync",), sems=("tn",))
                P.op("sync", lambda e, rt=rt, k0=k0, kw=kw: e.dma_start(
                    out=qq[:, 0:kw], in_=w3[rt, :, k0:k0 + kw]), inc="ld", n=16)
                P.wait("vector", "ld")
                P.wait("scalar", "ld")
                ve0 = P.counts["ve"]
                quant_core(qq[:, 0:kw], kw, qq[:, 0:kw], "plain")
                P.wait("vector", "tn")
                split_hl(qq, kw, qh_b, ql_b)
                emit_ham_keepalive(ve0, P.counts["ve"])
                P.wait("tensor", "ve")
                for kc in range(kw // 128):
                    bi = 6 + (kc % 2)
                    bank_write_ready(bi)
                    P.op("tensor", lambda e, kc=kc, bi=bi: e.transpose(
                        pbh[bi - 6][:, 0:128], qh_b[:, kc * 128:(kc + 1) * 128], identb_s[:]), inc="tn")
                    P.wait("scalar", "tn")
                    P.op("scalar", lambda e, kc=kc, bi=bi, k0=k0: e.copy(
                        strip_h[:, k0 // 128 + kc, :], pbh[bi - 6][:, 0:128]), inc="sl")
                    bank_consumed(bi, "sl")
                for kc in range(kw // 128):
                    bi = 6 + (kc % 2)
                    bank_write_ready(bi)
                    P.op("tensor", lambda e, kc=kc, bi=bi: e.transpose(
                        pbh[bi - 6][:, 0:128], ql_b[:, kc * 128:(kc + 1) * 128], identb_s[:]), inc="tn")
                    P.wait("scalar", "tn")
                    P.op("scalar", lambda e, kc=kc, bi=bi, k0=k0: e.copy(
                        strip_l[:, k0 // 128 + kc, :], pbh[bi - 6][:, 0:128]), inc="sl")
                    bank_consumed(bi, "sl")
            P.wait("gpsimd", "sl")
            P.op("gpsimd", lambda e, rt=rt, wT3h=wT3h, kcs=kcs: e.dma_start(
                out=wT3h[:, :, rt * 128:(rt + 1) * 128], in_=strip_h[:, 0:kcs, :]), inc="gd", n=16,
                group=("st", id(wTh_dram), rt))
            P.op("gpsimd", lambda e, rt=rt, wT3l=wT3l, kcs=kcs: e.dma_start(
                out=wT3l[:, :, rt * 128:(rt + 1) * 128], in_=strip_l[:, 0:kcs, :]), inc="gd", n=16,
                group=("st", id(wTh_dram), rt))

    if "f1" in phases or "f2" in phases:
        preprocess_weight(wg_d, IDIM, D, wgTqh_d, wgTql_d)
        preprocess_weight(wu_d, IDIM, D, wuTqh_d, wuTql_d)
        preprocess_weight(wd_d, D, IDIM, wdTqh_d, wdTql_d)
    P.barrier(engines=("sync",), sems=("gd",))

    wgT3h = wgTqh_d.rearrange("(c p) n -> p c n", p=128)
    wgT3l = wgTql_d.rearrange("(c p) n -> p c n", p=128)
    wuT3h = wuTqh_d.rearrange("(c p) n -> p c n", p=128)
    wuT3l = wuTql_d.rearrange("(c p) n -> p c n", p=128)
    wdT3h = wdTqh_d.rearrange("(c p) n -> p c n", p=128)
    wdT3l = wdTql_d.rearrange("(c p) n -> p c n", p=128)
    x3 = x_d.rearrange("(n p) k -> n p k", p=128)
    out3 = out_d.rearrange("(n p) k -> n p k", p=128)
    zp3 = zp_d.rearrange("(n p) k -> n p k", p=128)

    def ffn(seq, first):
        if first:
            P.barrier(engines=("sync",), sems=("ve", "sl"))
            for t in range(NT):
                P.op("sync", lambda e, seq=seq, t=t: e.dma_start(
                    out=z[:, t, :], in_=x3[seq * NT + t]), inc="ld", n=16, group=("x", seq))
            P.wait("vector", "ld")
            P.wait("scalar", "ld")
        batched_ln_stats()

        def prep(t):
            P.wait("vector", "tn")
            ve0 = P.counts["ve"]
            quant_ln_tile(z[:, t, :], qq[:], t)
            split_hl(qq, D, qh_b, ql_b)
            emit_ham_keepalive(ve0, P.counts["ve"])
            P.wait("tensor", "ve")
            transpose_bf16(qh_b, KC, xnqTh, t * 128)
            transpose_bf16(ql_b, KC, xnqTl, t * 128)
        def emit_gateup(t, nw, slot):
            gb, ub = (0, 1) if t % 2 == 0 else (2, 3)
            bank_write_ready(gb)
            bank_write_ready(ub)
            ts_ = t * 128
            for kc in range(KC):
                first, last = (kc == 0), (kc == KC - 1)
                for (bk, xop, wop, st, sp_) in (
                        (gb, xnqTh, wgb_h, first, False),
                        (gb, xnqTh, wgb_l, False, False),
                        (gb, xnqTl, wgb_h, False, last),
                        (ub, xnqTh, wub_h, first, False),
                        (ub, xnqTh, wub_l, False, False),
                        (ub, xnqTl, wub_h, False, last)):
                    P.op("tensor", lambda e, kc=kc, ts_=ts_, nw=nw, bk=bk, xop=xop,
                         wop=wop, st=st, sp_=sp_: e.matmul(
                        pb[bk][:, 0:nw], xop[:, kc, ts_:ts_ + 128],
                        wop[:, kc, 0:nw], start=st, stop=sp_),
                        inc="tn" if (sp_ and bk == ub) else None)
            return P.counts["tn"]

        def emit_tail(t, nw, nkc4, gu_done, slot):
            gb, ub = (0, 1) if t % 2 == 0 else (2, 3)
            P.wait("scalar", "tn", gu_done)
            P.wait("scalar", "ve")
            P.op("scalar", lambda e, nw=nw, gb=gb: e.activation(
                a_t[:, 0:nw], pb[gb][:, 0:nw], AF.Silu), inc="sl")
            bank_consumed(gb, "sl")
            P.wait("vector", "sl")
            P.wait("vector", "tn", gu_done)
            P.op("vector", lambda e, nw=nw, ub=ub: e.tensor_tensor(
                y_t[:, 0:nw], a_t[:, 0:nw], pb[ub][:, 0:nw], op=ALU.mult), inc="ve")
            bank_consumed(ub, "ve")
            quant_core(y_t[:, 0:nw], nw, qB[:, 0:nw], "plain")
            split_hl(qB, nw, hqh_b, hql_b)
            P.wait("tensor", "ve")
            transpose_bf16(hqh_b, nkc4, hqTh_b, 0)
            transpose_bf16(hql_b, nkc4, hqTl_b, 0)
            bank_write_ready(4)
            bank_write_ready(5)
            P.wait("tensor", "sl")
            for kc in range(nkc4):
                first, last = (kc == 0), (kc == nkc4 - 1)
                for (bk, c0, hop, wop, st, sp_) in (
                        (4, 0, hqTh_b, wdb_h, first, False),
                        (4, 0, hqTh_b, wdb_l, False, False),
                        (4, 0, hqTl_b, wdb_h, False, last),
                        (5, 512, hqTh_b, wdb_h, first, False),
                        (5, 512, hqTh_b, wdb_l, False, False),
                        (5, 512, hqTl_b, wdb_h, False, last)):
                    P.op("tensor", lambda e, kc=kc, bk=bk, c0=c0, hop=hop, wop=wop,
                         st=st, sp_=sp_: e.matmul(
                        pb[bk][:, :], hop[:, kc, :], wop[:, kc, c0:c0 + 512],
                        start=st, stop=sp_),
                        inc="tn" if (sp_ and bk == 5) else None)
            P.wait("vector", "tn")
            P.op("vector", lambda e, t=t: e.scalar_tensor_tensor(
                z[:, t, 0:512], pb[4][:, :], 0.5, z[:, t, 0:512],
                op0=ALU.mult, op1=ALU.add), inc="ve")
            bank_consumed(4, "ve")
            P.op("vector", lambda e, t=t: e.scalar_tensor_tensor(
                z[:, t, 512:1024], pb[5][:, :], 0.5, z[:, t, 512:1024],
                op0=ALU.mult, op1=ALU.add), inc="ve")
            bank_consumed(5, "ve")

        def emit_wload(isl):
            n0, nw = ISLICES[isl]
            nkc4 = nw // 128
            P.barrier(engines=("sync",), sems=("tn",))
            for (dst, src3, wide) in ((wgb_h, wgT3h, nw), (wgb_l, wgT3l, nw),
                                      (wub_h, wuT3h, nw), (wub_l, wuT3l, nw)):
                P.op("sync", lambda e, dst=dst, src3=src3, n0=n0, wide=wide: e.dma_start(
                    out=dst[:, :, 0:wide], in_=src3[:, :, n0:n0 + wide]), inc="ld", n=16,
                    group=("wl", isl))
            for (dst, src3) in ((wdb_h, wdT3h), (wdb_l, wdT3l)):
                P.op("sync", lambda e, dst=dst, src3=src3, n0=n0, nkc4=nkc4: e.dma_start(
                    out=dst[:, 0:nkc4, :], in_=src3[:, n0 // 128:n0 // 128 + nkc4, :]), inc="ld", n=16,
                    group=("wl", isl))
            return P.counts["ld"]

        wload_done = emit_wload(0)
        for t in range(NT):
            prep(t)
        for isl, (n0, nw) in enumerate(ISLICES):
            nkc4 = nw // 128
            if isl > 0:
                wload_done = emit_wload(isl)
            P.wait("tensor", "ld", wload_done)
            P.wait("tensor", "sl")
            gu_done = emit_gateup(0, nw, 0)
            for t in range(NT):
                nxt = emit_gateup(t + 1, nw, 0) if t + 1 < NT else None
                emit_tail(t, nw, nkc4, gu_done, 0)
                gu_done = nxt

    def attention(seq):
        for hp in range(2):
            P.barrier()
            P.op("sync", lambda e, hp=hp: e.dma_start(
                out=wqb[:], in_=wqT_d.rearrange("(c p) n -> p c n", p=128)[:, :, hp * 128:(hp + 1) * 128]), inc="ld", n=16, group=("aw", seq, hp))
            P.op("sync", lambda e, hp=hp: e.dma_start(
                out=wkb[:], in_=wkT_d.rearrange("(c p) n -> p c n", p=128)[:, :, hp * 128:(hp + 1) * 128]), inc="ld", n=16, group=("aw", seq, hp))
            P.op("sync", lambda e, hp=hp: e.dma_start(
                out=wvb[:], in_=wvT_d.rearrange("(c p) n -> p c n", p=128)[:, :, hp * 512:(hp + 1) * 512]), inc="ld", n=16, group=("aw", seq, hp))
            P.op("sync", lambda e, hp=hp: e.dma_start(
                out=wob[:], in_=woT_d.rearrange("(c p) n -> p c n", p=128)[:, hp * 4:(hp + 1) * 4, :]), inc="ld", n=16, group=("aw", seq, hp))
            P.op("vector", lambda e: e.memset(
                v65[:].rearrange("p t (h u) -> p t h u", u=65)[:, :, :, 64:65], 1.0), inc="ve")
            P.wait("tensor", "ld")
            if hp == 0:
                batched_ln_stats()
            for t in range(NT):
                ln_full(z[:, t, :], t)
                P.wait("tensor", "ve")
                P.wait("tensor", "sl")
                transpose_to(y_t[:], KC, yT_wide, 0, b0=2)
                P.wait("tensor", "sl")
                bank_write_ready(0)
                for kc in range(KC):
                    P.op("tensor", lambda e, kc=kc: e.matmul(
                        pb[0][:, 0:128], wkb[:, kc, :], yT_wide[:, kc, :],
                        start=(kc == 0), stop=(kc == KC - 1)),
                        inc="tn" if kc == KC - 1 else None)
                P.wait("scalar", "tn")
                P.op("scalar", lambda e, t=t: e.copy(
                    kst[:, t * 128:(t + 1) * 128], pb[0][:, 0:128]), inc="sl")
                bank_consumed(0, "sl")
                bank_write_ready(0)
                for kc in range(KC):
                    P.op("tensor", lambda e, kc=kc: e.matmul(
                        pb[0][:, 0:128], wqb[:, kc, :], yT_wide[:, kc, :],
                        start=(kc == 0), stop=(kc == KC - 1)),
                        inc="tn" if kc == KC - 1 else None)
                P.wait("scalar", "tn")
                P.op("scalar", lambda e, t=t: e.copy(
                    qst[:, t * 128:(t + 1) * 128], pb[0][:, 0:128]), inc="sl")
                bank_consumed(0, "sl")
                bank_write_ready(1)
                for kc in range(KC):
                    P.op("tensor", lambda e, kc=kc: e.matmul(
                        pb[1][:, :], yT_wide[:, kc, :], wvb[:, kc, :],
                        start=(kc == 0), stop=(kc == KC - 1)),
                        inc="tn" if kc == KC - 1 else None)
                P.wait("scalar", "tn")
                v65v = v65[:, t, :].rearrange("p (h u) -> p h u", u=65)
                P.op("scalar", lambda e, v65v=v65v: e.copy(
                    v65v[:, :, 0:64], pb[1][:, :].rearrange("p (h u) -> p h u", u=64)), inc="sl")
                bank_consumed(1, "sl")
            P.wait("sync", "sl")
            for h in range(8):
                g, c = h % 4, h // 4
                P.op("sync", lambda e, h=h, g=g, c=c: e.dma_start(
                    out=q32[32 * g:32 * g + 16, c, :], in_=qst[h * 16:(h + 1) * 16, :]), inc="ld", n=16,
                    group=("rgq", seq, hp))
            P.wait("sync", "ld")
            for h in range(8):
                g, c = h % 4, h // 4
                P.op("sync", lambda e, h=h, g=g, c=c: e.dma_start(
                    out=k32[32 * g:32 * g + 16, c, :], in_=kst[h * 16:(h + 1) * 16, :]), inc="ld", n=16,
                    group=("rgk", seq, hp))
            P.wait("sync", "sl")
            P.wait("sync", "ve")
            P.op("sync", lambda e: e.dma_start(out=triu_s, in_=triu_d), inc="ld", n=16)
            P.wait("tensor", "ld")
            P.wait("vector", "ld")
            aa8 = aa

            def emit_wave_qk(g, c, ti, w):
                tjs = list(range(4 * w, min(4 * w + 4, ti + 1)))
                sb_ = w % 2
                bank_write_ready(sb_)
                for k, tj in enumerate(tjs):
                    P.op("tensor", lambda e, g=g, c=c, ti=ti, tj=tj, k=k, sb_=sb_: e.matmul(
                        pb[sb_][:, k * 128:(k + 1) * 128],
                        k32[32 * g:32 * g + 16, c, tj * 128:(tj + 1) * 128],
                        q32[32 * g:32 * g + 16, c, ti * 128:(ti + 1) * 128],
                        start=True, stop=True, tile_position=(32 * g, 0)),
                        inc="tn" if k == len(tjs) - 1 else None)
                return P.counts["tn"]

            def emit_wave_poly_av(ti, h, w, qk_done, nb_):
                tjs = list(range(4 * w, min(4 * w + 4, ti + 1)))
                sb_ = w % 2
                P.wait("scalar", "tn", qk_done)
                for k, tj in enumerate(tjs):
                    aslot = (w % 2) * 4 + k
                    P.op("scalar", lambda e, k=k, sb_=sb_, aslot=aslot: e.activation(
                        aa8[:, aslot, :], pb[sb_][:, k * 128:(k + 1) * 128], AF.Square,
                        bias=bconst, scale=QPOLY_SCALE),
                        inc="sl")
                bank_consumed(sb_, "sl")
                P.wait("vector", "sl")
                for k, tj in enumerate(tjs):
                    aslot = (w % 2) * 4 + k
                    if tj == ti:
                        P.op("vector", lambda e, aslot=aslot: e.scalar_tensor_tensor(
                            aa8[:, aslot, :], aa8[:, aslot, :], 0.5, triu_s,
                            op0=ALU.add, op1=ALU.mult),
                            inc="ve" if k == len(tjs) - 1 else None)
                    else:
                        P.op("vector", lambda e, aslot=aslot: e.tensor_scalar(
                            aa8[:, aslot, :], aa8[:, aslot, :], 0.5, None, op0=ALU.add),
                            inc="ve" if k == len(tjs) - 1 else None)
                P.wait("tensor", "ve")
                for k, tj in enumerate(tjs):
                    aslot = (w % 2) * 4 + k
                    P.op("tensor", lambda e, tj=tj, ti=ti, h=h, nb_=nb_, aslot=aslot: e.matmul(
                        pb[nb_][:, 0:65], aa8[:, aslot, :],
                        v65[:, tj, h * 65:(h + 1) * 65],
                        start=(tj == 0), stop=(tj == ti)),
                        inc="tn" if tj == ti else None)

            for ti in range(NT):
                for hg in range(2):
                    for hh_ in range(4):
                        h = hg * 4 + hh_
                        g, c = h % 4, h // 4
                        nb_ = 2 + hh_
                        bank_write_ready(nb_)
                        nwv = (ti + 4) // 4
                        qk_done = emit_wave_qk(g, c, ti, 0)
                        for w in range(nwv):
                            nxt = emit_wave_qk(g, c, ti, w + 1) if w + 1 < nwv else None
                            emit_wave_poly_av(ti, h, w, qk_done, nb_)
                            qk_done = nxt
                    P.wait("vector", "tn")
                    for hh_ in range(4):
                        P.op("vector", lambda e, hh_=hh_: e.tensor_scalar(
                            small[:, 10 + hh_:11 + hh_], pb[2 + hh_][:, 64:65], 1e-12,
                            None, op0=ALU.add), inc="ve")
                    P.wait("vector", "ve")
                    P.op("vector", lambda e: e.reciprocal(bm_t[:, 0:4], small[:, 10:14]), inc="ve")
                    P.wait("vector", "ve")
                    for hh_ in range(4):
                        h = hg * 4 + hh_
                        P.op("vector", lambda e, hh_=hh_, h=h: e.tensor_scalar(
                            ystage[:, h * 64:(h + 1) * 64], pb[2 + hh_][:, 0:64],
                            bm_t[:, hh_:hh_ + 1], None, op0=ALU.mult), inc="ve")
                        bank_consumed(2 + hh_, "ve")
                P.wait("tensor", "ve")
                transpose_to(ystage[:], 4, yT, 0)
                bank_write_ready(4)
                bank_write_ready(5)
                P.wait("tensor", "sl")
                for kc in range(4):
                    P.op("tensor", lambda e, kc=kc: e.matmul(
                        pb[4][:, :], yT[:, kc, :], wob[:, kc, 0:512],
                        start=(kc == 0), stop=(kc == 3)), inc="tn" if kc == 3 else None)
                for kc in range(4):
                    P.op("tensor", lambda e, kc=kc: e.matmul(
                        pb[5][:, :], yT[:, kc, :], wob[:, kc, 512:1024],
                        start=(kc == 0), stop=(kc == 3)), inc="tn" if kc == 3 else None)
                if hp == 0:
                    P.wait("vector", "tn")
                    P.wait("vector", "gd")
                    P.op("vector", lambda e: e.tensor_copy(zpstage[:, 0:512], pb[4][:, :]), inc="ve")
                    bank_consumed(4, "ve")
                    P.op("vector", lambda e: e.tensor_copy(zpstage[:, 512:1024], pb[5][:, :]), inc="ve")
                    bank_consumed(5, "ve")
                    P.wait("gpsimd", "ve")
                    P.op("gpsimd", lambda e, ti=ti: e.dma_start(
                        out=zp3[ti], in_=zpstage), inc="gd", n=16)
                else:
                    P.wait("sync", "ve")
                    P.op("sync", lambda e, ti=ti: e.dma_start(
                        out=zpstage, in_=zp3[ti]), inc="ld", n=16)
                    P.wait("vector", "tn")
                    P.wait("vector", "ld")
                    P.op("vector", lambda e, ti=ti: e.tensor_tensor(
                        z[:, ti, 0:512], z[:, ti, 0:512], pb[4][:, :], op=ALU.add), inc="ve")
                    bank_consumed(4, "ve")
                    P.op("vector", lambda e, ti=ti: e.tensor_tensor(
                        z[:, ti, 512:1024], z[:, ti, 512:1024], pb[5][:, :], op=ALU.add), inc="ve")
                    bank_consumed(5, "ve")
                    P.op("vector", lambda e, ti=ti: e.tensor_tensor(
                        z[:, ti, :], z[:, ti, :], zpstage, op=ALU.add), inc="ve")

    def dump_z(seq):
        P.barrier(engines=("gpsimd",))
        for t in range(NT):
            P.op("gpsimd", lambda e, seq=seq, t=t: e.dma_start(
                out=out3[seq * NT + t], in_=z[:, t, :]), inc="gd", n=16, group=("zo", seq))

    def dump_xnq():
        P.barrier(engines=("gpsimd",))
        av = arenaA[:, 0:KC * T].rearrange("p (c t) -> p c t", c=NT)
        for i in range(NT):
            P.op("gpsimd", lambda e, i=i: e.dma_start(
                out=out3[i], in_=av[:, i, :]), inc="gd", n=16)

    def dump_wgtq(w_dram):
        P.barrier(engines=("gpsimd",))
        w3v = w_dram.rearrange("(c p) n -> c p n", p=128)
        nch = w3v.shape[0]
        ww = w3v.shape[2]
        ndump = NT * n_seq
        njp = (ww + D - 1) // D
        for c in range(nch):
            if (c + 1) * njp > ndump:
                break
            for j in range((ww + D - 1) // D):
                wj = min(D, ww - j * D)
                P.op("gpsimd", lambda e, c=c, j=j, wj=wj, w3v=w3v, njp=njp: e.dma_start(
                    out=out3[c * njp + j][:, 0:wj], in_=w3v[c, :, j * D:j * D + wj]), inc="gd", n=16)

    for seq in range(n_seq):
        P.barrier()
        if "f1" in phases:
            ffn(seq, first=True)
        else:
            for t in range(NT):
                P.op("sync", lambda e, seq=seq, t=t: e.dma_start(
                    out=z[:, t, :], in_=x3[seq * NT + t]), inc="ld", n=16, group=("x", seq))
            P.barrier()
        if debug_out == "z1":
            dump_z(seq)
            continue
        if debug_out in ("xnq", "wgtq", "wdtq") and seq == 0:
            if debug_out == "xnq":
                dump_xnq()
            elif debug_out == "wgtq":
                dump_wgtq(wgTq_d)
            else:
                dump_wgtq(wdTq_d)
            break
        if "attn" in phases:
            attention(seq)
        if debug_out == "z2":
            dump_z(seq)
            continue
        P.barrier()
        if "f2" in phases:
            ffn(seq, first=False)
        dump_z(seq)

    P.wait("gpsimd", "gd")

    with nc.Block() as block:
        P.materialize(block)
    stack.close()
    return nc


def kernel(L0_attn_hidden_states, L0_attn_Wq, L0_attn_Wk, L0_attn_Wv, L0_attn_Wo,
           L0_ffn_w_gate, L0_ffn_w_up, L0_ffn_w_down,
           _phases=("f1", "attn", "f2"), _debug_out=None):
    x = np.ascontiguousarray(np.asarray(L0_attn_hidden_states, dtype=np.float32))
    B, L, Dm = x.shape
    n_cores = 8
    spc = B // n_cores
    common = {
        "wqT": np.ascontiguousarray(np.asarray(L0_attn_Wq, np.float32).T),
        "wkT": np.ascontiguousarray(np.asarray(L0_attn_Wk, np.float32).T),
        "wvT": np.ascontiguousarray(np.asarray(L0_attn_Wv, np.float32).T),
        "woT": np.ascontiguousarray(np.asarray(L0_attn_Wo, np.float32).T),
        "wg": np.ascontiguousarray(np.asarray(L0_ffn_w_gate, np.float32)),
        "wu": np.ascontiguousarray(np.asarray(L0_ffn_w_up, np.float32)),
        "wd": np.ascontiguousarray(np.asarray(L0_ffn_w_down, np.float32)),
        "ident": np.eye(128, dtype=np.float32),
        "identb": np.eye(128).astype(ml_dtypes.bfloat16),
        "triu": np.triu(np.ones((128, 128), dtype=np.float32)),
    }
    nc = build_core_program(n_seq=spc, phases=_phases, debug_out=_debug_out)
    in_maps = []
    for c in range(n_cores):
        m = dict(common)
        m["x"] = np.ascontiguousarray(x[c * spc:(c + 1) * spc].reshape(spc * L, Dm))
        in_maps.append(m)
    res = run_bass_kernel_spmd(nc, in_maps, core_ids=list(range(n_cores)))
    outs = [np.asarray(res.results[c]["out"]).reshape(spc, L, Dm) for c in range(n_cores)]
    return np.concatenate(outs, axis=0)
